# version 02
# baseline (speedup 1.0000x reference)
import numpy as np
import ml_dtypes
import contextlib
import concourse.bass as bass
import concourse.mybir as mybir
from concourse.bass_utils import run_bass_kernel_spmd

F32 = mybir.dt.float32
BF16 = mybir.dt.bfloat16
AF = mybir.ActivationFunctionType
ALU = mybir.AluOpType
AX = mybir.AxisListType

D = 2048
T = 2048
KT = D // 128
DE = 1408
MT = DE // 128
OFT = D // 128
NTC = T // 512
NE = 8
NH = 16
EPS = 1e-5
SM_SCALE = float(1.0 / np.sqrt(128.0))

STAGES = 3
DEBUG = None
EXPERTS = None

ENGINES = ("tensor", "vector", "scalar", "gpsimd", "sync")


def build_program(stages=3):
    nc = bass.Bass("TRN2", target_bir_lowering=False)

    def inp(name, shape, dt=F32):
        return nc.dram_tensor(name, list(shape), dt, kind="ExternalInput").ap()

    xT = inp("xT", [D, T])
    wqkvT = inp("wqkvT", [D, 3 * D])
    qkvb = inp("qkvb", [128, 48])
    woutT = inp("woutT", [D, D])
    outb = inp("outb", [128, 16])
    wr = inp("wr", [D, NE])
    weg = inp("weg", [NE, D, DE])
    weu = inp("weu", [NE, D, DE])
    wed = inp("wed", [NE, DE, D])
    wsg = inp("wsg", [D, 2 * DE])
    wsu = inp("wsu", [D, 2 * DE])
    wsd = inp("wsd", [2 * DE, D])
    identB = inp("identB", [128, 128], BF16)

    outT = nc.dram_tensor("outT", [D, T], F32, kind="ExternalOutput").ap()
    x_hbm = nc.dram_tensor("x_hbm", [D, T], F32, kind="Internal").ap()
    qT_hbm = nc.dram_tensor("qT_hbm", [D, T], BF16, kind="Internal").ap()
    kT_hbm = nc.dram_tensor("kT_hbm", [D, T], BF16, kind="Internal").ap()
    combT_hbm = nc.dram_tensor("combT_hbm", [NE, T], BF16, kind="Internal").ap()

    wqkvT_v = wqkvT.rearrange("(kt p) n -> p kt n", p=128)
    woutT_v = woutT.rearrange("(kt p) n -> p kt n", p=128)
    wr_v = wr.rearrange("(kt p) e -> p kt e", p=128)
    weg_v = weg.rearrange("e (kt p) n -> e p kt n", p=128)
    weu_v = weu.rearrange("e (kt p) n -> e p kt n", p=128)
    wed_v = wed.rearrange("e (mt p) n -> e p mt n", p=128)
    wsg_v = wsg.rearrange("(kt p) n -> p kt n", p=128)
    wsu_v = wsu.rearrange("(kt p) n -> p kt n", p=128)
    wsd_v = wsd.rearrange("(hh mt p) n -> hh p mt n", hh=2, p=128)

    B0_ = (nc.sbuf_base + 31) // 32 * 32
    A = nc.alloc_sbuf_tensor_at
    sY = A("sY", [128, KT * T], BF16, offset=B0_ + 0)
    sF = A("sF", [128, KT * T], BF16, offset=B0_ + 65536)
    UN = B0_ + 131072
    sH = A("sH", [128, MT * T], BF16, offset=UN)
    qkst = [A(f"qkst{j}", [128, T], BF16, offset=UN + 4096 * j) for j in range(2)]
    vtst = A("vtst", [128, T], BF16, offset=UN + 8192)
    kh = [A(f"kh{j}", [128, T], BF16, offset=UN + 4096 * j) for j in range(2)]
    qh = [A(f"qh{j}", [128, T], BF16, offset=UN + 8192 + 4096 * j) for j in range(2)]
    pt = [A(f"pt{j}", [128, 1024], BF16, offset=UN + 16384 + 2048 * j) for j in range(2)]
    sumacc = A("sumacc", [128, 1024], F32, offset=UN + 20480)
    rbcsb = A("rbcsb", [128, 512], BF16, offset=UN + 24576)
    xst = [A(f"xst{j}", [128, T], F32, offset=UN + 8192 * j) for j in range(2)]
    sqt = [A(f"sqt{j}", [128, 512], BF16, offset=UN + 16384 + 1024 * j) for j in range(2)]
    murow = A("murow", [1, 512], F32, offset=UN + 18432)
    rrow = A("rrow", [1, 512], F32, offset=UN + 26624)

    W0 = B0_ + 176128
    sWg = [A(f"sWg{j}", [128, KT * 128], BF16, offset=W0 + 4096 * j) for j in range(2)]
    sWu = [A(f"sWu{j}", [128, KT * 128], BF16, offset=W0 + 8192 + 4096 * j) for j in range(2)]
    sWd = [A(f"sWd{j}", [128, MT * 128], BF16, offset=W0 + 16384 + 2816 * j) for j in range(2)]
    sCbc = A("sCbc", [128, T], BF16, offset=W0 + 22016)
    sSg = [A(f"sSg{j}", [128, 512], BF16, offset=W0 + 26112 + 1024 * j) for j in range(2)]
    sCrow = A("sCrow", [1, 512], BF16, offset=W0 + 28160)
    SM0 = W0 + 29184
    sQkvb = A("sQkvb", [128, 48], F32, offset=SM0)
    sOutb = A("sOutb", [128, 16], F32, offset=SM0 + 192)
    sWr = A("sWr", [128, KT * NE], BF16, offset=SM0 + 256)
    sIdB = A("sIdB", [128, 128], BF16, offset=SM0 + 512)
    sOnesP = A("sOnesP", [128, 1], BF16, offset=SM0 + 768)
    sOnesPF = A("sOnesPF", [128, 1], F32, offset=SM0 + 800)
    sOnesR = A("sOnesR", [1, 128], BF16, offset=SM0 + 832)
    sOnesRF = A("sOnesRF", [1, 128], F32, offset=SM0 + 1088)
    sCtm = A("sCtm", [128, 128], BF16, offset=SM0 + 1600)
    sProb = A("sProb", [128, 8], F32, offset=SM0 + 1856)
    sMask = A("sMask", [128, 8], F32, offset=SM0 + 1888)
    sPm = A("sPm", [128, 8], F32, offset=SM0 + 1920)
    sNm = A("sNm", [128, 1], F32, offset=SM0 + 1952)
    sRsum = A("sRsum", [128, 1], F32, offset=SM0 + 1984)
    sRc = A("sRc", [128, 1], F32, offset=SM0 + 2016)
    sM1 = A("sM1", [128, 1], F32, offset=SM0 + 2048)
    sM2 = A("sM2", [128, 1], F32, offset=SM0 + 2080)
    cEps = A("cEps", [128, 1], F32, offset=SM0 + 2112)
    cNh = A("cNh", [128, 1], F32, offset=SM0 + 2144)
    cSms = A("cSms", [128, 1], F32, offset=SM0 + 2176)
    sCts = A("sCts", [8, 128], BF16, offset=SM0 + 2208)

    with contextlib.ExitStack() as es:
        E = lambda t: es.enter_context(t)
        PS = [E(nc.psum_tensor(f"PS{i}", [128, 512], F32)) for i in range(7)]
        PTb = E(nc.psum_tensor("PTb", [128, 1024], BF16))
        sems = {n: E(nc.semaphore(n)) for n in ("pe", "act", "dve", "dg", "ds")}
        blk = E(nc.Block())

        q = {e: [] for e in ENGINES}
        C = {n: 0 for n in sems}

        def op(eng, method, *args, waits=(), inc=None, **kw):
            n = None
            if inc is not None:
                C[inc] += 16 if method.startswith("dma") else 1
                n = C[inc]
            ws = tuple((s, v) for (s, v) in waits if v and v > 0)
            q[eng].append((method, args, kw, ws, (inc, n)))
            return n

        NOFREE = ("pe", 0)
        bank_free = [NOFREE] * 7
        ptb_free_t = [NOFREE]
        wg_free = [0, 0]; wu_free = [0, 0]; wd_free = [0, 0]
        sg_free = [NOFREE, NOFREE]
        xst_ds_free = [0, 0]; xst_dve_free = [0, 0]
        sqt_free = [0, 0]
        crow_free = 0
        qkst_free = [0, 0]
        vt_free = 0
        kh_free = [0, 0]; qh_free = [0, 0]
        pt_free = [(0, 0), (0, 0)]

        op("sync", "dma_start", sQkvb[:], qkvb[:], inc="ds")
        op("sync", "dma_start", sOutb[:], outb[:], inc="ds")
        op("sync", "dma_start", sIdB[:], identB[:], inc="ds")
        CONST_DS = C["ds"]
        op("gpsimd", "dma_start", sWr[:].rearrange("p (kt e) -> p kt e", e=NE),
           wr_v[:], inc="dg")
        op("gpsimd", "memset", sOnesP[:], 1.0)
        op("gpsimd", "memset", sOnesPF[:], 1.0)
        op("gpsimd", "memset", sOnesR[:], 1.0)
        op("gpsimd", "memset", sOnesRF[:], 1.0)
        op("gpsimd", "memset", cEps[:], EPS)
        op("gpsimd", "memset", cNh[:], -0.5)
        op("gpsimd", "memset", cSms[:], SM_SCALE, inc="dg")
        CONST_DG = C["dg"]

        def boundary(src, dst, coef, do_ln, f_done, first=False):
            entry_pe, entry_dve, entry_act = C["pe"], C["dve"], C["act"]
            xhat_dve = [0] * KT
            for ft in range(KT):
                j = ft % 2
                w = [("ds", xst_ds_free[j]), ("dve", xst_dve_free[j])]
                if ft < 2:
                    w += [("pe", entry_pe), ("dve", entry_dve), ("act", entry_act)]
                    if f_done:
                        w.append(f_done)
                ld = op("sync", "dma_start", xst[j][:],
                        src[ft * 128:(ft + 1) * 128, :], waits=w, inc="ds")
                if coef is not None:
                    n = op("vector", "scalar_tensor_tensor", xst[j][:],
                           sF[:, ft * T:(ft + 1) * T], coef, xst[j][:],
                           ALU.mult, ALU.add, waits=[("ds", ld)], inc="dve")
                    done = [("dve", n)]
                else:
                    done = [("ds", ld)]
                if dst is not None:
                    st = op("sync", "dma_start", dst[ft * 128:(ft + 1) * 128, :],
                            xst[j][:], waits=done, inc="ds")
                    xst_ds_free[j] = st
                if do_ln:
                    n = op("vector", "tensor_copy", sY[:, ft * T:(ft + 1) * T],
                           xst[j][:], waits=done if coef is None else (), inc="dve")
                    xhat_dve[ft] = n
                    xst_dve_free[j] = n
                elif coef is not None:
                    xst_dve_free[j] = n
            if not do_ln:
                return
            for tc in range(NTC):
                for ft in range(KT):
                    j = ft % 2
                    ysl = sY[:, ft * T + tc * 512: ft * T + tc * 512 + 512]
                    na = op("scalar", "activation", sqt[j][:], ysl, AF.Square,
                            waits=[("dve", xhat_dve[ft]), ("pe", sqt_free[j]),
                                   ("dg", CONST_DG if (first and ft < 2 and tc == 0)
                                    else 0)], inc="act")
                    w = [("act", na)]
                    if ft == 0:
                        w += [bank_free[0], bank_free[1]]
                        if first and tc == 0:
                            w += [("dg", CONST_DG), ("ds", CONST_DS)]
                    op("tensor", "matmul", PS[0][0:1, :], sOnesP[:], ysl,
                       start=(ft == 0), stop=(ft == KT - 1), skip_group_check=True,
                       waits=w)
                    npe = op("tensor", "matmul", PS[1][0:1, :], sOnesP[:], sqt[j][:],
                             start=(ft == 0), stop=(ft == KT - 1),
                             skip_group_check=True, inc="pe")
                    sqt_free[j] = npe
                sums_pe = C["pe"]
                op("vector", "tensor_scalar", murow[:], PS[0][0:1, :], 1.0 / D, None,
                   ALU.mult, waits=[("pe", sums_pe)])
                op("vector", "tensor_mul", rrow[:], murow[:], murow[:])
                n = op("vector", "scalar_tensor_tensor", rrow[:], PS[1][0:1, :],
                       1.0 / D, rrow[:], ALU.mult, ALU.subtract, inc="dve")
                bank_free[0] = ("dve", n); bank_free[1] = ("dve", n)
                op("scalar", "activation", rrow[:], rrow[:], AF.Ln,
                   bias=cEps[0:1, :], waits=[("dve", n)])
                na = op("scalar", "activation", rrow[:], rrow[:], AF.Exp,
                        scale=cNh[0:1, :], inc="act")
                w = [("act", na), bank_free[2], bank_free[3]]
                op("tensor", "matmul", PS[2][:], sOnesRF[:], murow[:],
                   start=True, stop=True, skip_group_check=True, waits=w)
                npe = op("tensor", "matmul", PS[3][:], sOnesRF[:], rrow[:],
                         start=True, stop=True, skip_group_check=True, inc="pe")
                for ft in range(KT):
                    ysl = sY[:, ft * T + tc * 512: ft * T + tc * 512 + 512]
                    op("vector", "tensor_tensor", ysl, ysl, PS[2][:], ALU.subtract,
                       waits=[("pe", npe)] if ft == 0 else ())
                    n = op("vector", "tensor_tensor", ysl, ysl, PS[3][:], ALU.mult,
                           inc="dve")
                bank_free[2] = ("dve", n); bank_free[3] = ("dve", n)
            return C["dve"]

        def ffn_stage(y_ready):
            nonlocal crow_free
            entry_ds, entry_pe = C["ds"], C["pe"]
            rs_act = 0
            router_ds_end = 0
            for tt in range(KT):
                w = [("act", rs_act), bank_free[6], ("dve", y_ready)]
                if tt == 0:
                    w.append(("dg", CONST_DG))
                for kt in range(KT):
                    npe = op("tensor", "matmul", PS[6][:, 0:8],
                             sY[:, kt * T + tt * 128: kt * T + tt * 128 + 128],
                             sWr[:, kt * NE:(kt + 1) * NE],
                             start=(kt == 0), stop=(kt == KT - 1),
                             skip_group_check=True, waits=w if kt == 0 else (),
                             inc="pe" if kt == KT - 1 else None)
                n = op("vector", "reduce_max", sNm[:], PS[6][:, 0:8], AX.X,
                       negate=True, waits=[("pe", npe)], inc="dve")
                rs_act = op("scalar", "activation", sProb[:], PS[6][:, 0:8], AF.Exp,
                            bias=sNm[:], accum_out=sRsum[:], waits=[("dve", n)],
                            inc="act")
                bank_free[6] = ("act", rs_act)
                def vch(method, *args, extra=(), **kw):
                    nn = op("vector", method, *args,
                            waits=[("dve", C["dve"])] + list(extra), inc="dve", **kw)
                    return nn
                vch("reciprocal", sRc[:], sRsum[:], extra=[("act", rs_act)])
                vch("tensor_scalar", sProb[:], sProb[:], sRc[:], None, ALU.mult)
                vch("reduce_max", sM1[:], sProb[:], AX.X)
                vch("tensor_scalar", sMask[:], sProb[:], sM1[:], None, ALU.is_ge)
                vch("tensor_scalar", sMask[:], sMask[:], -1.0, 1.0,
                    ALU.mult, ALU.add)
                vch("tensor_mul", sPm[:], sProb[:], sMask[:])
                vch("reduce_max", sM2[:], sPm[:], AX.X)
                vch("tensor_scalar", sMask[:], sProb[:], sM2[:], None, ALU.is_ge)
                ctm = sCtm[:, tt * 8:(tt + 1) * 8]
                n = vch("tensor_mul", ctm, sProb[:], sMask[:])
                npe = op("tensor", "transpose", PTb[0:8, 0:128], ctm, sIdB[:],
                         waits=[("dve", n), ptb_free_t[0],
                                ("ds", CONST_DS if tt == 0 else 0)], inc="pe")
                n = op("vector", "tensor_copy", sCts[:], PTb[0:8, 0:128],
                       waits=[("pe", npe)], inc="dve")
                ptb_free_t[0] = ("dve", n)
                router_ds_end = op("sync", "dma_start",
                                   combT_hbm[:, tt * 128:(tt + 1) * 128], sCts[:],
                                   waits=[("dve", n)], inc="ds")

            if DEBUG == "rtr":
                for row, tn in [(0, sProb), (1, sMask), (2, sPm)]:
                    op("gpsimd", "dma_start", outT[row * 128:(row + 1) * 128, 0:8],
                       tn[:], waits=[("dve", C["dve"])], inc="dg")
                op("gpsimd", "dma_start", outT[384:512, 0:128], sCtm[:],
                   waits=[("dve", C["dve"])], inc="dg")
                for col, tn in [(0, sNm), (1, sRsum), (2, sRc), (3, sM1), (4, sM2)]:
                    op("gpsimd", "dma_start",
                       outT[512 + 128 * col:640 + 128 * col, 0:1], tn[:],
                       waits=[("dve", C["dve"])], inc="dg",
                       allow_slow_non_contiguous=True)
                return ("dve", C["dve"])
            first_union_write = True
            si_list = list(range(10)) if EXPERTS is None else list(EXPERTS)
            for si in si_list:
                shared = si < 2
                first = si == si_list[0]
                if shared:
                    hh = si
                    gv = lambda m, hh=hh: wsg_v[:, :, hh * DE + m * 128:
                                                hh * DE + (m + 1) * 128]
                    uv = lambda m, hh=hh: wsu_v[:, :, hh * DE + m * 128:
                                                hh * DE + (m + 1) * 128]
                    dv = lambda of, hh=hh: wsd_v[hh, :, :, of * 128:(of + 1) * 128]
                else:
                    e = si - 2
                    gv = lambda m, e=e: weg_v[e, :, :, m * 128:(m + 1) * 128]
                    uv = lambda m, e=e: weu_v[e, :, :, m * 128:(m + 1) * 128]
                    dv = lambda of, e=e: wed_v[e, :, :, of * 128:(of + 1) * 128]
                    for tc in range(NTC):
                        ld = op("sync", "dma_start", sCrow[:],
                                combT_hbm[e:e + 1, tc * 512:(tc + 1) * 512],
                                waits=[("ds", router_ds_end), ("pe", crow_free)],
                                inc="ds")
                        crow_free = op("tensor", "matmul", PS[6][:], sOnesR[:],
                                       sCrow[:], start=True, stop=True,
                                       skip_group_check=True,
                                       waits=[("ds", ld), bank_free[6]], inc="pe")
                        n = op("vector", "tensor_copy",
                               sCbc[:, tc * 512:(tc + 1) * 512], PS[6][:],
                               waits=[("pe", crow_free)], inc="dve")
                        bank_free[6] = ("dve", n)
                    if DEBUG == "cbc0":
                        op("gpsimd", "dma_start", outT[0:128, :], sCbc[:],
                           waits=[("dve", n)], inc="dg")
                        op("gpsimd", "dma_start", outT[128:256, 0:128], sCtm[:],
                           waits=[("dve", n)], inc="dg")
                        return ("dve", C["dve"])
                for m in range(MT):
                    j = m % 2
                    op("gpsimd", "dma_start",
                       sWg[j][:].rearrange("p (kt n) -> p kt n", n=128), gv(m),
                       waits=[("pe", wg_free[j])], inc="dg")
                    w_ready = op("gpsimd", "dma_start",
                                 sWu[j][:].rearrange("p (kt n) -> p kt n", n=128),
                                 uv(m), waits=[("pe", wu_free[j])], inc="dg")
                    for tc in range(NTC):
                        jj = (m * NTC + tc) % 2
                        w = [bank_free[2 * jj], bank_free[2 * jj + 1],
                             ("dve", y_ready)]
                        if tc == 0:
                            w.append(("dg", w_ready))
                        for kt in range(KT):
                            op("tensor", "matmul", PS[2 * jj][:],
                               sWg[j][:, kt * 128:(kt + 1) * 128],
                               sY[:, kt * T + tc * 512: kt * T + tc * 512 + 512],
                               start=(kt == 0), stop=(kt == KT - 1),
                               skip_group_check=True, waits=w if kt == 0 else ())
                        for kt in range(KT):
                            gp = op("tensor", "matmul", PS[2 * jj + 1][:],
                                    sWu[j][:, kt * 128:(kt + 1) * 128],
                                    sY[:, kt * T + tc * 512: kt * T + tc * 512 + 512],
                                    start=(kt == 0), stop=(kt == KT - 1),
                                    skip_group_check=True,
                                    inc="pe" if kt == KT - 1 else None)
                        na = op("scalar", "activation", sSg[jj][:], PS[2 * jj][:],
                                AF.Silu, waits=[("pe", gp), sg_free[jj]], inc="act")
                        hsl = sH[:, m * T + tc * 512: m * T + tc * 512 + 512]
                        hw = [("act", na)]
                        if first_union_write:
                            hw += [("ds", entry_ds), ("pe", entry_pe)]
                            first_union_write = False
                        if shared:
                            n = op("vector", "tensor_mul", hsl, sSg[jj][:],
                                   PS[2 * jj + 1][:], waits=hw, inc="dve")
                        else:
                            op("vector", "tensor_mul", hsl, sSg[jj][:],
                               PS[2 * jj + 1][:], waits=hw)
                            n = op("vector", "tensor_mul", hsl, hsl,
                                   sCbc[:, tc * 512:(tc + 1) * 512], inc="dve")
                        bank_free[2 * jj] = ("dve", n)
                        bank_free[2 * jj + 1] = ("dve", n)
                        sg_free[jj] = ("dve", n)
                    wg_free[j] = C["pe"]
                    wu_free[j] = C["pe"]
                h_done = C["dve"]
                for of in range(OFT):
                    j = of % 2
                    w_ready = op("gpsimd", "dma_start",
                                 sWd[j][:].rearrange("p (mt n) -> p mt n", n=128),
                                 dv(of), waits=[("pe", wd_free[j])], inc="dg")
                    for tc in range(NTC):
                        jj = (of * NTC + tc) % 2
                        bs, bv = bank_free[4 + jj]
                        w = [(bs, bv), ("dve", h_done)]
                        if tc == 0:
                            w.append(("dg", w_ready))
                        for mt in range(MT):
                            gp = op("tensor", "matmul", PS[4 + jj][:],
                                    sWd[j][:, mt * 128:(mt + 1) * 128],
                                    sH[:, mt * T + tc * 512: mt * T + tc * 512 + 512],
                                    start=(mt == 0), stop=(mt == MT - 1),
                                    skip_group_check=True, waits=w if mt == 0 else (),
                                    inc="pe" if mt == MT - 1 else None)
                        fsl = sF[:, of * T + tc * 512: of * T + tc * 512 + 512]
                        if first:
                            n = op("vector", "tensor_copy", fsl, PS[4 + jj][:],
                                   waits=[("pe", gp)], inc="dve")
                        else:
                            n = op("vector", "tensor_add", fsl, fsl, PS[4 + jj][:],
                                   waits=[("pe", gp)], inc="dve")
                        bank_free[4 + jj] = ("dve", n)
                    wd_free[j] = C["pe"]
            return ("dve", C["dve"])

        def attn_stage(y_ready):
            nonlocal vt_free
            entry_pe, entry_dve, entry_act = C["pe"], C["dve"], C["act"]
            qkst_free[0] = xst_ds_free[0]
            qkst_free[1] = xst_ds_free[1]
            bank_free_a = [bank_free[0], bank_free[1]]
            a_sync_writes = 0
            v_done = 0
            for of in range(48):
                j = of % 2
                w_ready = op("gpsimd", "dma_start",
                             sWg[j][:].rearrange("p (kt n) -> p kt n", n=128),
                             wqkvT_v[:, :, of * 128:(of + 1) * 128],
                             waits=[("pe", wg_free[j])], inc="dg")
                last_act = 0
                for tc in range(NTC):
                    jj = (of * NTC + tc) % 2
                    w = [bank_free_a[jj], ("dve", y_ready)]
                    if tc == 0:
                        w.append(("dg", w_ready))
                    for kt in range(KT):
                        gp = op("tensor", "matmul", PS[jj][:],
                                sWg[j][:, kt * 128:(kt + 1) * 128],
                                sY[:, kt * T + tc * 512: kt * T + tc * 512 + 512],
                                start=(kt == 0), stop=(kt == KT - 1),
                                skip_group_check=True, waits=w if kt == 0 else (),
                                inc="pe" if kt == KT - 1 else None)
                    if of < 32:
                        dest = qkst[j][:, tc * 512:(tc + 1) * 512]
                        dw = [("pe", gp)]
                        if tc == 0:
                            dw.append(("ds", qkst_free[j]))
                    else:
                        dest = vtst[:, tc * 512:(tc + 1) * 512]
                        dw = [("pe", gp)]
                        if tc == 0:
                            dw += [("pe", vt_free), ("ds", xst_ds_free[1])]
                    last_act = op("scalar", "activation", dest, PS[jj][:], AF.Identity,
                                  bias=sQkvb[:, of:of + 1], waits=dw, inc="act")
                    bank_free_a[jj] = ("act", last_act)
                if of < 32:
                    tgt = qT_hbm if of < 16 else kT_hbm
                    r = of % 16
                    n = op("sync", "dma_start", tgt[r * 128:(r + 1) * 128, :],
                           qkst[j][:], waits=[("act", last_act)], inc="ds")
                    qkst_free[j] = n
                    a_sync_writes = n
                else:
                    v = of - 32
                    for half in range(2):
                        w = [("act", last_act), ptb_free_t[0]]
                        for j8 in range(8):
                            tile = half * 8 + j8
                            gp = op("tensor", "transpose",
                                    PTb[:, j8 * 128:(j8 + 1) * 128],
                                    vtst[:, tile * 128:(tile + 1) * 128], sIdB[:],
                                    waits=w if j8 == 0 else (),
                                    inc="pe" if j8 == 7 else None)
                        dst = sF[:].rearrange("p (tt c) -> p tt c", c=T)[
                            :, half * 8:(half + 1) * 8, v * 128:(v + 1) * 128]
                        src = PTb[:].rearrange("p (x c) -> p x c", c=128)
                        vw = [("pe", gp)]
                        if v == 0 and half == 0:
                            vw += [("dve", entry_dve)]
                        n = op("vector", "tensor_copy", dst, src, waits=vw,
                               inc="dve")
                        ptb_free_t[0] = ("dve", n)
                    vt_free = C["pe"]
                    v_done = C["dve"]
                wg_free[j] = C["pe"]

            if DEBUG == "aqk":
                op("gpsimd", "dma_start", outT[0:1024, :], qT_hbm[0:1024, :],
                   waits=[("ds", a_sync_writes)], inc="dg")
                op("gpsimd", "dma_start", outT[1024:2048, :], kT_hbm[0:1024, :],
                   waits=[("ds", a_sync_writes)], inc="dg")
                return ("dve", C["dve"])
            if DEBUG == "av":
                for ft in range(KT):
                    op("gpsimd", "dma_start", outT[ft * 128:(ft + 1) * 128, :],
                       sF[:, ft * T:(ft + 1) * T], waits=[("dve", v_done)], inc="dg")
                return ("dve", C["dve"])

            s_bank = [bank_free_a[0], bank_free_a[1],
                      ("pe", 0), ("pe", 0)]
            cx_free = [bank_free[4], bank_free[5]]
            p6_free = bank_free[6]
            sumacc_free = 0
            phase_a_pe, phase_a_act = C["pe"], C["act"]
            for h in range(NH):
                j = h % 2
                lk = op("sync", "dma_start", kh[j][:],
                        kT_hbm[h * 128:(h + 1) * 128, :],
                        waits=[("ds", a_sync_writes), ("pe", kh_free[j]),
                               ("dve", entry_dve if h < 2 else 0),
                               ("pe", phase_a_pe if h < 2 else 0),
                               ("act", phase_a_act if h < 2 else 0)], inc="ds")
                lq = op("sync", "dma_start", qh[j][:],
                        qT_hbm[h * 128:(h + 1) * 128, :],
                        waits=[("pe", qh_free[j])], inc="ds")
                for tqc in range(2):
                    for tk in range(16):
                        jj = tk % 2
                        w = [("ds", lq), s_bank[2 * jj], s_bank[2 * jj + 1]]
                        if tk < 2:
                            w += [cx_free[0], cx_free[1], ("dve", v_done)]
                        for half in range(2):
                            gp = op("tensor", "matmul", PS[2 * jj + half][:],
                                    kh[j][:, tk * 128:(tk + 1) * 128],
                                    qh[j][:, tqc * 1024 + half * 512:
                                           tqc * 1024 + (half + 1) * 512],
                                    start=True, stop=True, skip_group_check=True,
                                    waits=w if half == 0 else (),
                                    inc="pe" if half == 1 else None)
                        ew = [("pe", gp), ("dve", pt_free[jj][0]),
                              ("pe", pt_free[jj][1])]
                        for half in range(2):
                            na = op("scalar", "activation",
                                    pt[jj][:, half * 512:(half + 1) * 512],
                                    PS[2 * jj + half][:], AF.Exp, scale=cSms[:],
                                    waits=ew if half == 0 else (),
                                    inc="act" if half == 1 else None)
                        s_bank[2 * jj] = ("act", na)
                        s_bank[2 * jj + 1] = ("act", na)
                        if tk == 0:
                            n = op("vector", "tensor_copy", sumacc[:], pt[jj][:],
                                   waits=[("act", na), ("pe", sumacc_free)],
                                   inc="dve")
                        else:
                            n = op("vector", "tensor_add", sumacc[:], sumacc[:],
                                   pt[jj][:], waits=[("act", na)], inc="dve")
                        vv = sF[:].rearrange("p (tt c) -> p tt c", c=T)
                        for half in range(2):
                            gp = op("tensor", "matmul", PS[4 + half][:],
                                    vv[:, tk, h * 128:(h + 1) * 128],
                                    pt[jj][:, half * 512:(half + 1) * 512],
                                    start=(tk == 0), stop=(tk == 15),
                                    skip_group_check=True,
                                    waits=[("act", na)] if half == 0 else (),
                                    inc="pe" if half == 1 else None)
                        pt_free[jj] = (n, gp)
                    ctx_pe = C["pe"]
                    sums_dve = C["dve"]
                    for half in range(2):
                        w = [("dve", sums_dve), p6_free]
                        npe = op("tensor", "matmul", PS[6][0:1, :], sOnesPF[:],
                                 sumacc[:, half * 512:(half + 1) * 512],
                                 start=True, stop=True, skip_group_check=True,
                                 waits=w, inc="pe")
                        sumacc_free = npe
                        n = op("vector", "reciprocal", rrow[:], PS[6][0:1, :],
                               waits=[("pe", npe)], inc="dve")
                        npe = op("tensor", "matmul", PS[6][:], sOnesRF[:], rrow[:],
                                 start=True, stop=True, skip_group_check=True,
                                 waits=[("dve", n)], inc="pe")
                        n = op("vector", "tensor_copy", rbcsb[:], PS[6][:],
                               waits=[("pe", npe)], inc="dve")
                        p6_free = ("dve", n)
                        ysl = sY[:, h * T + tqc * 1024 + half * 512:
                                 h * T + tqc * 1024 + (half + 1) * 512]
                        n = op("vector", "tensor_tensor", ysl, PS[4 + half][:],
                               rbcsb[:], ALU.mult, waits=[("pe", ctx_pe)],
                               inc="dve")
                        cx_free[half] = ("dve", n)
                kh_free[j] = C["pe"]
                qh_free[j] = C["pe"]
            b_end_dve = C["dve"]
            bank_free[4] = cx_free[0]
            bank_free[5] = cx_free[1]
            bank_free[6] = p6_free
            if DEBUG == "actx":
                for ft in range(KT):
                    op("gpsimd", "dma_start", outT[ft * 128:(ft + 1) * 128, :],
                       sY[:, ft * T:(ft + 1) * T], waits=[("dve", b_end_dve)],
                       inc="dg")
                return ("dve", C["dve"])

            bank_free_c = [s_bank[0], s_bank[1]]
            for of in range(OFT):
                j = of % 2
                w_ready = op("gpsimd", "dma_start",
                             sWg[j][:].rearrange("p (kt n) -> p kt n", n=128),
                             woutT_v[:, :, of * 128:(of + 1) * 128],
                             waits=[("pe", wg_free[j])], inc="dg")
                for tc in range(NTC):
                    jj = (of * NTC + tc) % 2
                    w = [bank_free_c[jj], ("dve", b_end_dve)]
                    if tc == 0:
                        w.append(("dg", w_ready))
                    for kt in range(KT):
                        gp = op("tensor", "matmul", PS[jj][:],
                                sWg[j][:, kt * 128:(kt + 1) * 128],
                                sY[:, kt * T + tc * 512: kt * T + tc * 512 + 512],
                                start=(kt == 0), stop=(kt == KT - 1),
                                skip_group_check=True, waits=w if kt == 0 else (),
                                inc="pe" if kt == KT - 1 else None)
                    fsl = sF[:, of * T + tc * 512: of * T + tc * 512 + 512]
                    na = op("scalar", "activation", fsl, PS[jj][:], AF.Identity,
                            bias=sOutb[:, of:of + 1],
                            waits=[("pe", gp), ("dve", b_end_dve)], inc="act")
                    bank_free_c[jj] = ("act", na)
                wg_free[j] = C["pe"]
            bank_free[0] = bank_free_c[0]
            bank_free[1] = bank_free_c[1]
            bank_free[2] = s_bank[2]
            bank_free[3] = s_bank[3]
            return ("act", C["act"])

        def dump(buf, wait):
            for ft in range(KT):
                op("gpsimd", "dma_start", outT[ft * 128:(ft + 1) * 128, :],
                   buf[:, ft * T:(ft + 1) * T], waits=[wait], inc="dg")

        y1 = boundary(xT, None, None, True, None, first=True)
        if DEBUG == "y1":
            dump(sY, ("dve", y1))
            q["sync"].append(("wait_ge", (sems["dg"], C["dg"]), {}, (), (None, None)))
        elif DEBUG in ("f1", "cbc0", "rtr"):
            f1 = ffn_stage(y1)
            if DEBUG == "f1":
                dump(sF, f1)
            q["sync"].append(("wait_ge", (sems["dg"], C["dg"]), {}, (), (None, None)))
        else:
            f1 = ffn_stage(y1)
        if DEBUG in ("y1", "f1", "cbc0", "rtr"):
            pass
        elif stages == 1:
            boundary(xT, outT, 0.5, False, f1)
        else:
            y2 = boundary(xT, x_hbm, 0.5, True, f1)
            f2 = attn_stage(y2)
            if DEBUG in ("aqk", "av", "actx"):
                q["sync"].append(("wait_ge", (sems["dg"], C["dg"]), {}, (),
                                  (None, None)))
            elif stages == 2:
                boundary(x_hbm, outT, 1.0, False, f2)
            else:
                y3 = boundary(x_hbm, x_hbm, 1.0, True, f2)
                f3 = ffn_stage(y3)
                boundary(x_hbm, outT, 0.5, False, f3)
        q["sync"].append(("wait_ge", (sems["ds"], C["ds"]), {}, (), (None, None)))

        def make_body(items):
            def body(e):
                for method, args, kw, waits, (inc, n) in items:
                    for sname, v in waits:
                        e.wait_ge(sems[sname], v)
                    if method == "wait_ge":
                        e.wait_ge(*args)
                        continue
                    inst = getattr(e, method)(*args, **kw)
                    if inc is not None:
                        inst.then_inc(sems[inc],
                                      16 if method.startswith("dma") else 1)
            return body

        for eng in ENGINES:
            getattr(blk, eng)(make_body(q[eng]))
    return nc


def kernel(L0_attn_hidden_states, L0_attn_Wqkv_w, L0_attn_Wqkv_b, L0_attn_out_w,
           L0_attn_out_b, L0_ffn_W_router, L0_ffn_We_gate, L0_ffn_We_up,
           L0_ffn_We_down, L0_ffn_Ws_gate, L0_ffn_Ws_up, L0_ffn_Ws_down):
    x = np.asarray(L0_attn_hidden_states, dtype=np.float32)
    f32 = lambda a: np.asarray(a, dtype=np.float32)
    wqkvT = np.ascontiguousarray(f32(L0_attn_Wqkv_w).T)
    qkvb = np.ascontiguousarray(f32(L0_attn_Wqkv_b).reshape(48, 128).T)
    woutT = np.ascontiguousarray(f32(L0_attn_out_w).T)
    outb = np.ascontiguousarray(f32(L0_attn_out_b).reshape(16, 128).T)
    identB = np.eye(128, dtype=ml_dtypes.bfloat16)

    nc = build_program(STAGES)
    shared = dict(wqkvT=wqkvT, qkvb=qkvb, woutT=woutT, outb=outb,
                  wr=np.ascontiguousarray(f32(L0_ffn_W_router)),
                  weg=np.ascontiguousarray(f32(L0_ffn_We_gate)),
                  weu=np.ascontiguousarray(f32(L0_ffn_We_up)),
                  wed=np.ascontiguousarray(f32(L0_ffn_We_down)),
                  wsg=np.ascontiguousarray(f32(L0_ffn_Ws_gate)),
                  wsu=np.ascontiguousarray(f32(L0_ffn_Ws_up)),
                  wsd=np.ascontiguousarray(f32(L0_ffn_Ws_down)),
                  identB=identB)
    in_maps = [dict(xT=np.ascontiguousarray(x[b].T), **shared) for b in range(8)]
    res = run_bass_kernel_spmd(nc, in_maps, core_ids=list(range(8)))
    out = np.stack([np.asarray(r["outT"], dtype=np.float32).T
                    for r in res.results], axis=0)
    return out


# version 04
# speedup vs baseline: 1.0849x; 1.0849x over previous
import numpy as np
import ml_dtypes
import contextlib
import concourse.bass as bass
import concourse.mybir as mybir
from concourse.bass_utils import run_bass_kernel_spmd

F32 = mybir.dt.float32
BF16 = mybir.dt.bfloat16
AF = mybir.ActivationFunctionType
ALU = mybir.AluOpType
AX = mybir.AxisListType

D = 2048
T = 2048
KT = D // 128
DE = 1408
MT = DE // 128
OFT = D // 128
NTC = T // 512
NE = 8
NH = 16
EPS = 1e-5
SM_SCALE = float(1.0 / np.sqrt(128.0))

STAGES = 3
DEBUG = None
EXPERTS = None

ENGINES = ("tensor", "vector", "scalar", "gpsimd", "sync")


def build_program(stages=3):
    nc = bass.Bass("TRN2", target_bir_lowering=False)

    def inp(name, shape, dt=F32):
        return nc.dram_tensor(name, list(shape), dt, kind="ExternalInput").ap()

    xT = inp("xT", [D, T])
    wegu = inp("wegu", [NE, MT, 128, 2 * KT * 128])
    wedt = inp("wedt", [NE, OFT, 128, MT * 128])
    wsgu = inp("wsgu", [2, MT, 128, 2 * KT * 128])
    wsdt = inp("wsdt", [2, OFT, 128, MT * 128])
    wqkvt = inp("wqkvt", [48, 128, KT * 128])
    woutt = inp("woutt", [OFT, 128, KT * 128])
    qkvb = inp("qkvb", [128, 48])
    outb = inp("outb", [128, 16])
    wr = inp("wr", [D, NE])
    identB = inp("identB", [128, 128], BF16)

    outT = nc.dram_tensor("outT", [D, T], F32, kind="ExternalOutput").ap()
    x_hbm = nc.dram_tensor("x_hbm", [D, T], F32, kind="Internal").ap()
    qT_hbm = nc.dram_tensor("qT_hbm", [D, T], BF16, kind="Internal").ap()
    kT_hbm = nc.dram_tensor("kT_hbm", [D, T], BF16, kind="Internal").ap()
    combT_hbm = nc.dram_tensor("combT_hbm", [NE, T], BF16, kind="Internal").ap()

    wr_v = wr.rearrange("(kt p) e -> p kt e", p=128)

    B0_ = (nc.sbuf_base + 31) // 32 * 32
    A = nc.alloc_sbuf_tensor_at
    sY = A("sY", [128, KT * T], BF16, offset=B0_ + 0)
    sF = A("sF", [128, KT * T], BF16, offset=B0_ + 65536)
    UN = B0_ + 131072
    sH = A("sH", [128, MT * T], BF16, offset=UN)
    qkst = [A(f"qkst{j}", [128, T], BF16, offset=UN + 4096 * j) for j in range(2)]
    vtst = A("vtst", [128, T], BF16, offset=UN + 8192)
    kh = [A(f"kh{j}", [128, T], BF16, offset=UN + 4096 * j) for j in range(2)]
    qh = [A(f"qh{j}", [128, T], BF16, offset=UN + 8192 + 4096 * j) for j in range(2)]
    pt = [A(f"pt{j}", [128, 1024], BF16, offset=UN + 16384 + 2048 * j) for j in range(2)]
    sumacc = A("sumacc", [128, 1024], F32, offset=UN + 20480)
    rbcsb = A("rbcsb", [128, 512], BF16, offset=UN + 24576)
    xst = [A(f"xst{j}", [128, T], F32, offset=UN + 8192 * j) for j in range(2)]
    sqt = [A(f"sqt{j}", [128, 512], BF16, offset=UN + 16384 + 1024 * j) for j in range(2)]
    murow = A("murow", [1, 512], F32, offset=UN + 18432)
    rrow = A("rrow", [1, 512], F32, offset=UN + 26624)

    W0 = B0_ + 176128
    sWgu = [A(f"sWgu{j}", [128, 2 * KT * 128], BF16, offset=W0 + 8192 * j)
            for j in range(2)]
    sWg = [sWgu[0][:, 0:2048], sWgu[1][:, 0:2048]]
    sWu = [sWgu[0][:, 2048:4096], sWgu[1][:, 2048:4096]]
    sWd = [A(f"sWd{j}", [128, MT * 128], BF16, offset=W0 + 16384 + 2816 * j) for j in range(2)]
    sCbc = A("sCbc", [128, T], BF16, offset=W0 + 22016)
    sSg = [A(f"sSg{j}", [128, 512], BF16, offset=W0 + 26112 + 1024 * j) for j in range(2)]
    sCrow = A("sCrow", [1, 512], BF16, offset=W0 + 28160)
    SM0 = W0 + 29184
    sQkvb = A("sQkvb", [128, 48], F32, offset=SM0)
    sOutb = A("sOutb", [128, 16], F32, offset=SM0 + 192)
    sWr = A("sWr", [128, KT * NE], BF16, offset=SM0 + 256)
    sIdB = A("sIdB", [128, 128], BF16, offset=SM0 + 512)
    sOnesP = A("sOnesP", [128, 1], BF16, offset=SM0 + 768)
    sOnesPF = A("sOnesPF", [128, 1], F32, offset=SM0 + 800)
    sOnesR = A("sOnesR", [1, 128], BF16, offset=SM0 + 832)
    sOnesRF = A("sOnesRF", [1, 128], F32, offset=SM0 + 1088)
    sCtm = A("sCtm", [128, 128], BF16, offset=SM0 + 1600)
    sProb = A("sProb", [128, 8], F32, offset=SM0 + 1856)
    sMask = A("sMask", [128, 8], F32, offset=SM0 + 1888)
    sPm = A("sPm", [128, 8], F32, offset=SM0 + 1920)
    sNm = A("sNm", [128, 1], F32, offset=SM0 + 1952)
    sRsum = A("sRsum", [128, 1], F32, offset=SM0 + 1984)
    sRc = A("sRc", [128, 1], F32, offset=SM0 + 2016)
    sM1 = A("sM1", [128, 1], F32, offset=SM0 + 2048)
    sM2 = A("sM2", [128, 1], F32, offset=SM0 + 2080)
    cEps = A("cEps", [128, 1], F32, offset=SM0 + 2112)
    cNh = A("cNh", [128, 1], F32, offset=SM0 + 2144)
    cSms = A("cSms", [128, 1], F32, offset=SM0 + 2176)
    sCts = A("sCts", [8, 128], BF16, offset=SM0 + 2208)

    with contextlib.ExitStack() as es:
        E = lambda t: es.enter_context(t)
        PSA = E(nc.psum_tensor("PSA", [128, 1024], F32))
        PSB = E(nc.psum_tensor("PSB", [128, 1024], F32))
        PSC = E(nc.psum_tensor("PSC", [128, 512], F32))
        PSD = E(nc.psum_tensor("PSD", [128, 512], F32))
        P6 = E(nc.psum_tensor("P6", [128, 512], F32))
        PS = [PSA[:, 0:512], PSA[:, 512:1024], PSB[:, 0:512], PSB[:, 512:1024],
              PSC[:], PSD[:], P6[:]]
        PTb = E(nc.psum_tensor("PTb", [128, 1024], BF16))
        sems = {n: E(nc.semaphore(n)) for n in
                ("pe", "act", "dve", "dg", "ds", "pool",
                 "wg0", "wg1", "wd0", "wd1", "xs0", "xs1",
                 "qs0", "qs1", "kq0", "kq1", "cr", "ct")}
        blk = E(nc.Block())

        q = {e: [] for e in ENGINES}
        C = {n: 0 for n in sems}

        def op(eng, method, *args, waits=(), inc=None, **kw):
            n = None
            if inc is not None:
                C[inc] += 16 if method.startswith("dma") else 1
                n = C[inc]
            ws = tuple((s, v) for (s, v) in waits if v and v > 0)
            q[eng].append((method, args, kw, ws, (inc, n)))
            return n

        NOFREE = ("pe", 0)
        bank_free = [NOFREE] * 7
        ptb_free_t = [NOFREE]
        wg_free = [0, 0]; wu_free = [0, 0]; wd_free = [0, 0]
        sg_free = [NOFREE, NOFREE]
        xst_ds_free = [0, 0]; xst_dve_free = [0, 0]
        sqt_free = [0, 0]
        crow_free = 0
        qkst_free = [0, 0]
        vt_free = 0
        kh_free = [0, 0]; qh_free = [0, 0]
        pt_free = [(0, 0), (0, 0)]

        op("sync", "dma_start", sQkvb[:], qkvb[:], inc="ds")
        op("sync", "dma_start", sOutb[:], outb[:], inc="ds")
        op("sync", "dma_start", sIdB[:], identB[:], inc="ds")
        CONST_DS = C["ds"]
        op("gpsimd", "dma_start", sWr[:].rearrange("p (kt e) -> p kt e", e=NE),
           wr_v[:], inc="dg")
        op("gpsimd", "memset", sOnesP[:], 1.0)
        op("gpsimd", "memset", sOnesPF[:], 1.0)
        op("gpsimd", "memset", sOnesR[:], 1.0)
        op("gpsimd", "memset", sOnesRF[:], 1.0)
        op("gpsimd", "memset", cEps[:], EPS)
        op("gpsimd", "memset", cNh[:], -0.5)
        op("gpsimd", "memset", cSms[:], SM_SCALE, inc="dg")
        CONST_DG = C["dg"]

        def boundary(src, dst, coef, do_ln, f_done, first=False):
            entry_pe, entry_dve, entry_act = C["pe"], C["dve"], C["act"]
            xhat_dve = [0] * KT
            for ft in range(KT):
                j = ft % 2
                w = [(f"xs{j}", xst_ds_free[j]), ("dve", xst_dve_free[j])]
                if ft < 2:
                    w += [("pe", entry_pe), ("dve", entry_dve), ("act", entry_act)]
                    if f_done:
                        w.append(f_done)
                ld = op("sync", "dma_start", xst[j][:],
                        src[ft * 128:(ft + 1) * 128, :], waits=w, inc=f"xs{j}")
                if coef is not None:
                    n = op("vector", "scalar_tensor_tensor", xst[j][:],
                           sF[:, ft * T:(ft + 1) * T], coef, xst[j][:],
                           ALU.mult, ALU.add, waits=[(f"xs{j}", ld)], inc="dve")
                    done = [("dve", n)]
                else:
                    done = [(f"xs{j}", ld)]
                if dst is not None:
                    st = op("sync", "dma_start", dst[ft * 128:(ft + 1) * 128, :],
                            xst[j][:], waits=done, inc=f"xs{j}")
                    xst_ds_free[j] = st
                if do_ln:
                    n = op("vector", "tensor_copy", sY[:, ft * T:(ft + 1) * T],
                           xst[j][:], waits=done if coef is None else (), inc="dve")
                    xhat_dve[ft] = n
                    xst_dve_free[j] = n
                elif coef is not None:
                    xst_dve_free[j] = n
            if not do_ln:
                return
            for tc in range(NTC):
                for ft in range(KT):
                    j = ft % 2
                    ysl = sY[:, ft * T + tc * 512: ft * T + tc * 512 + 512]
                    na = op("scalar", "activation", sqt[j][:], ysl, AF.Square,
                            waits=[("dve", xhat_dve[ft]), ("pe", sqt_free[j]),
                                   ("dg", CONST_DG if (first and ft < 2 and tc == 0)
                                    else 0)], inc="act")
                    w = [("act", na)]
                    if ft == 0:
                        w += [bank_free[0], bank_free[1]]
                        if first and tc == 0:
                            w += [("dg", CONST_DG), ("ds", CONST_DS)]
                    op("tensor", "matmul", PS[0][0:1, :], sOnesP[:], ysl,
                       start=(ft == 0), stop=(ft == KT - 1), skip_group_check=True,
                       waits=w)
                    npe = op("tensor", "matmul", PS[1][0:1, :], sOnesP[:], sqt[j][:],
                             start=(ft == 0), stop=(ft == KT - 1),
                             skip_group_check=True, inc="pe")
                    sqt_free[j] = npe
                sums_pe = C["pe"]
                op("vector", "tensor_scalar", murow[:], PS[0][0:1, :], 1.0 / D, None,
                   ALU.mult, waits=[("pe", sums_pe)])
                op("vector", "tensor_mul", rrow[:], murow[:], murow[:])
                n = op("vector", "scalar_tensor_tensor", rrow[:], PS[1][0:1, :],
                       1.0 / D, rrow[:], ALU.mult, ALU.subtract, inc="dve")
                bank_free[0] = ("dve", n); bank_free[1] = ("dve", n)
                op("scalar", "activation", rrow[:], rrow[:], AF.Ln,
                   bias=cEps[0:1, :], waits=[("dve", n)])
                na = op("scalar", "activation", rrow[:], rrow[:], AF.Exp,
                        scale=cNh[0:1, :], inc="act")
                w = [("act", na), bank_free[2], bank_free[3]]
                op("tensor", "matmul", PS[2][:], sOnesRF[:], murow[:],
                   start=True, stop=True, skip_group_check=True, waits=w)
                npe = op("tensor", "matmul", PS[3][:], sOnesRF[:], rrow[:],
                         start=True, stop=True, skip_group_check=True, inc="pe")
                for ft in range(KT):
                    ysl = sY[:, ft * T + tc * 512: ft * T + tc * 512 + 512]
                    op("vector", "tensor_tensor", ysl, ysl, PS[2][:], ALU.subtract,
                       waits=[("pe", npe)] if ft == 0 else ())
                    n = op("vector", "tensor_tensor", ysl, ysl, PS[3][:], ALU.mult,
                           inc="dve")
                bank_free[2] = ("dve", n); bank_free[3] = ("dve", n)
            return C["dve"]

        def ffn_stage(y_ready):
            entry_xs0, entry_xs1, entry_pe = C["xs0"], C["xs1"], C["pe"]
            rs_act = 0
            router_ds_end = 0
            for tt in range(KT):
                w = [("act", rs_act), bank_free[6], ("dve", y_ready)]
                if tt == 0:
                    w.append(("dg", CONST_DG))
                for kt in range(KT):
                    npe = op("tensor", "matmul", PS[6][:, 0:8],
                             sY[:, kt * T + tt * 128: kt * T + tt * 128 + 128],
                             sWr[:, kt * NE:(kt + 1) * NE],
                             start=(kt == 0), stop=(kt == KT - 1),
                             skip_group_check=True, waits=w if kt == 0 else (),
                             inc="pe" if kt == KT - 1 else None)
                n = op("vector", "reduce_max", sNm[:], PS[6][:, 0:8], AX.X,
                       negate=True, waits=[("pe", npe)], inc="dve")
                rs_act = op("scalar", "activation", sProb[:], PS[6][:, 0:8], AF.Exp,
                            bias=sNm[:], accum_out=sRsum[:], waits=[("dve", n)],
                            inc="act")
                bank_free[6] = ("act", rs_act)
                def vch(method, *args, extra=(), **kw):
                    nn = op("vector", method, *args,
                            waits=[("dve", C["dve"])] + list(extra), inc="dve", **kw)
                    return nn
                vch("reciprocal", sRc[:], sRsum[:], extra=[("act", rs_act)])
                vch("tensor_scalar", sProb[:], sProb[:], sRc[:], None, ALU.mult)
                vch("reduce_max", sM1[:], sProb[:], AX.X)
                vch("tensor_scalar", sMask[:], sProb[:], sM1[:], None, ALU.is_ge)
                vch("tensor_scalar", sMask[:], sMask[:], -1.0, 1.0,
                    ALU.mult, ALU.add)
                vch("tensor_mul", sPm[:], sProb[:], sMask[:])
                vch("reduce_max", sM2[:], sPm[:], AX.X)
                vch("tensor_scalar", sMask[:], sProb[:], sM2[:], None, ALU.is_ge)
                ctm = sCtm[:, tt * 8:(tt + 1) * 8]
                n = vch("tensor_mul", ctm, sProb[:], sMask[:])
                npe = op("tensor", "transpose", PTb[0:8, 0:128], ctm, sIdB[:],
                         waits=[("dve", n), ptb_free_t[0],
                                ("ds", CONST_DS if tt == 0 else 0)], inc="pe")
                n = op("vector", "tensor_copy", sCts[:], PTb[0:8, 0:128],
                       waits=[("pe", npe)], inc="dve")
                ptb_free_t[0] = ("dve", n)
                router_ds_end = op("sync", "dma_start",
                                   combT_hbm[:, tt * 128:(tt + 1) * 128], sCts[:],
                                   waits=[("dve", n)], inc="ct")

            if DEBUG == "rtr":
                for row, tn in [(0, sProb), (1, sMask), (2, sPm)]:
                    op("gpsimd", "dma_start", outT[row * 128:(row + 1) * 128, 0:8],
                       tn[:], waits=[("dve", C["dve"])], inc="dg")
                op("gpsimd", "dma_start", outT[384:512, 0:128], sCtm[:],
                   waits=[("dve", C["dve"])], inc="dg")
                for col, tn in [(0, sNm), (1, sRsum), (2, sRc), (3, sM1), (4, sM2)]:
                    op("gpsimd", "dma_start",
                       outT[512 + 128 * col:640 + 128 * col, 0:1], tn[:],
                       waits=[("dve", C["dve"])], inc="dg",
                       allow_slow_non_contiguous=True)
                return ("dve", C["dve"])
            first_union_write = True
            si_list = list(range(10)) if EXPERTS is None else list(EXPERTS)
            cbc_built = set()

            def emit_cbc(e):
                nonlocal crow_free
                for tc in range(NTC):
                    ld = op("sync", "dma_start", sCrow[:],
                            combT_hbm[e:e + 1, tc * 512:(tc + 1) * 512],
                            waits=[("ct", router_ds_end), ("pe", crow_free)],
                            inc="cr")
                    crow_free = op("tensor", "matmul", PS[6][:], sOnesR[:],
                                   sCrow[:], start=True, stop=True,
                                   skip_group_check=True,
                                   waits=[("cr", ld), bank_free[6]], inc="pe")
                    n = op("vector", "tensor_copy",
                           sCbc[:, tc * 512:(tc + 1) * 512], PS[6][:],
                           waits=[("pe", crow_free)], inc="dve")
                    bank_free[6] = ("dve", n)

            for idx, si in enumerate(si_list):
                shared = si < 2
                first = si == si_list[0]
                if shared:
                    hh = si
                    guv = lambda m, hh=hh: wsgu[hh, m]
                    dv = lambda of, hh=hh: wsdt[hh, of]
                else:
                    e = si - 2
                    guv = lambda m, e=e: wegu[e, m]
                    dv = lambda of, e=e: wedt[e, of]
                    if si not in cbc_built:
                        emit_cbc(si - 2)
                        cbc_built.add(si)
                    if DEBUG == "cbc0":
                        op("gpsimd", "dma_start", outT[0:128, :], sCbc[:],
                           waits=[("dve", n)], inc="dg")
                        op("gpsimd", "dma_start", outT[128:256, 0:128], sCtm[:],
                           waits=[("dve", n)], inc="dg")
                        return ("dve", C["dve"])
                for m in range(MT):
                    j = m % 2
                    w_ready = op("gpsimd", "dma_start", sWgu[j][:], guv(m),
                                 waits=[("pe", wg_free[j])], inc=f"wg{j}")
                    for tc in range(NTC):
                        jj = (m * NTC + tc) % 2
                        w = [bank_free[2 * jj], bank_free[2 * jj + 1],
                             ("dve", y_ready)]
                        if tc == 0:
                            w.append((f"wg{j}", w_ready))
                        for kt in range(KT):
                            op("tensor", "matmul", PS[2 * jj][:],
                               sWgu[j][:, kt * 128:(kt + 1) * 128],
                               sY[:, kt * T + tc * 512: kt * T + tc * 512 + 512],
                               start=(kt == 0), stop=(kt == KT - 1),
                               skip_group_check=True, waits=w if kt == 0 else ())
                        for kt in range(KT):
                            gp = op("tensor", "matmul", PS[2 * jj + 1][:],
                                    sWgu[j][:, 2048 + kt * 128: 2048 + (kt + 1) * 128],
                                    sY[:, kt * T + tc * 512: kt * T + tc * 512 + 512],
                                    start=(kt == 0), stop=(kt == KT - 1),
                                    skip_group_check=True,
                                    inc="pe" if kt == KT - 1 else None)
                        na = op("scalar", "activation", sSg[jj][:], PS[2 * jj][:],
                                AF.Silu, waits=[("pe", gp), sg_free[jj]], inc="act")
                        hsl = sH[:, m * T + tc * 512: m * T + tc * 512 + 512]
                        hw = [("act", na)]
                        if first_union_write:
                            hw += [("xs0", entry_xs0), ("xs1", entry_xs1),
                                   ("pe", entry_pe)]
                            first_union_write = False
                        if shared:
                            n = op("vector", "tensor_mul", hsl, sSg[jj][:],
                                   PS[2 * jj + 1][:], waits=hw, inc="dve")
                        else:
                            op("vector", "tensor_mul", hsl, sSg[jj][:],
                               PS[2 * jj + 1][:], waits=hw)
                            n = op("vector", "tensor_mul", hsl, hsl,
                                   sCbc[:, tc * 512:(tc + 1) * 512], inc="dve")
                        bank_free[2 * jj] = ("dve", n)
                        bank_free[2 * jj + 1] = ("dve", n)
                        sg_free[jj] = ("dve", n)
                    wg_free[j] = C["pe"]
                h_done = C["dve"]
                nxt = si_list[idx + 1] if idx + 1 < len(si_list) else None
                if nxt is not None and nxt >= 2 and nxt not in cbc_built:
                    emit_cbc(nxt - 2)
                    cbc_built.add(nxt)
                for of in range(OFT):
                    j = of % 2
                    w_ready = op("gpsimd", "dma_start", sWd[j][:], dv(of),
                                 waits=[("pe", wd_free[j])], inc=f"wd{j}")
                    for tc in range(NTC):
                        jj = (of * NTC + tc) % 2
                        bs, bv = bank_free[4 + jj]
                        w = [(bs, bv), ("dve", h_done)]
                        if tc == 0:
                            w.append((f"wd{j}", w_ready))
                        for mt in range(MT):
                            gp = op("tensor", "matmul", PS[4 + jj][:],
                                    sWd[j][:, mt * 128:(mt + 1) * 128],
                                    sH[:, mt * T + tc * 512: mt * T + tc * 512 + 512],
                                    start=(mt == 0), stop=(mt == MT - 1),
                                    skip_group_check=True, waits=w if mt == 0 else (),
                                    inc="pe" if mt == MT - 1 else None)
                        fsl = sF[:, of * T + tc * 512: of * T + tc * 512 + 512]
                        if first:
                            n = op("vector", "tensor_copy", fsl, PS[4 + jj][:],
                                   waits=[("pe", gp)], inc="dve")
                        else:
                            n = op("vector", "tensor_add", fsl, fsl, PS[4 + jj][:],
                                   waits=[("pe", gp)], inc="dve")
                        bank_free[4 + jj] = ("dve", n)
                    wd_free[j] = C["pe"]
            return ("dve", C["dve"])

        def attn_stage(y_ready):
            nonlocal vt_free
            entry_pe, entry_dve, entry_act = C["pe"], C["dve"], C["act"]
            qkst_free[0] = 0
            qkst_free[1] = 0
            entry_xs = [xst_ds_free[0], xst_ds_free[1]]
            bank_free_a = [bank_free[0], bank_free[1]]
            a_sync_writes = 0
            v_done = 0
            for of in range(48):
                j = of % 2
                w_ready = op("gpsimd", "dma_start", sWgu[j][:, 0:2048],
                             wqkvt[of], waits=[("pe", wg_free[j])], inc=f"wg{j}")
                last_act = 0
                for tc in range(NTC):
                    jj = (of * NTC + tc) % 2
                    w = [bank_free_a[jj], ("dve", y_ready)]
                    if tc == 0:
                        w.append((f"wg{j}", w_ready))
                    for kt in range(KT):
                        gp = op("tensor", "matmul", PS[jj][:],
                                sWgu[j][:, kt * 128:(kt + 1) * 128],
                                sY[:, kt * T + tc * 512: kt * T + tc * 512 + 512],
                                start=(kt == 0), stop=(kt == KT - 1),
                                skip_group_check=True, waits=w if kt == 0 else (),
                                inc="pe" if kt == KT - 1 else None)
                    if of < 32:
                        dest = qkst[j][:, tc * 512:(tc + 1) * 512]
                        dw = [("pe", gp)]
                        if tc == 0:
                            dw.append((f"qs{j}", qkst_free[j]))
                        if of < 2 and tc == 0:
                            dw.append((f"xs{j}", entry_xs[j]))
                    else:
                        dest = vtst[:, tc * 512:(tc + 1) * 512]
                        dw = [("pe", gp)]
                        if tc == 0:
                            dw += [("pe", vt_free), ("xs1", xst_ds_free[1])]
                    last_act = op("scalar", "activation", dest, PS[jj][:], AF.Identity,
                                  bias=sQkvb[:, of:of + 1], waits=dw, inc="act")
                    bank_free_a[jj] = ("act", last_act)
                if of < 32:
                    tgt = qT_hbm if of < 16 else kT_hbm
                    r = of % 16
                    n = op("sync", "dma_start", tgt[r * 128:(r + 1) * 128, :],
                           qkst[j][:], waits=[("act", last_act)], inc=f"qs{j}")
                    qkst_free[j] = n
                else:
                    v = of - 32
                    for half in range(2):
                        w = [("act", last_act), ptb_free_t[0]]
                        for j8 in range(8):
                            tile = half * 8 + j8
                            gp = op("tensor", "transpose",
                                    PTb[:, j8 * 128:(j8 + 1) * 128],
                                    vtst[:, tile * 128:(tile + 1) * 128], sIdB[:],
                                    waits=w if j8 == 0 else (),
                                    inc="pe" if j8 == 7 else None)
                        dst = sF[:].rearrange("p (tt c) -> p tt c", c=T)[
                            :, half * 8:(half + 1) * 8, v * 128:(v + 1) * 128]
                        src = PTb[:].rearrange("p (x c) -> p x c", c=128)
                        vw = [("pe", gp)]
                        if v == 0 and half == 0:
                            vw += [("dve", entry_dve)]
                        n = op("vector", "tensor_copy", dst, src, waits=vw,
                               inc="dve")
                        ptb_free_t[0] = ("dve", n)
                    vt_free = C["pe"]
                    v_done = C["dve"]
                wg_free[j] = C["pe"]

            if DEBUG == "aqk":
                op("gpsimd", "dma_start", outT[0:1024, :], qT_hbm[0:1024, :],
                   waits=[("ds", a_sync_writes)], inc="dg")
                op("gpsimd", "dma_start", outT[1024:2048, :], kT_hbm[0:1024, :],
                   waits=[("ds", a_sync_writes)], inc="dg")
                return ("dve", C["dve"])
            if DEBUG == "av":
                for ft in range(KT):
                    op("gpsimd", "dma_start", outT[ft * 128:(ft + 1) * 128, :],
                       sF[:, ft * T:(ft + 1) * T], waits=[("dve", v_done)], inc="dg")
                return ("dve", C["dve"])

            s_bank = [bank_free_a[0], bank_free_a[1]]
            cx_free = [bank_free[4], bank_free[5]]
            p6_free = bank_free[6]
            sumacc_free = 0
            phase_a_pe, phase_a_act = C["pe"], C["act"]
            for h in range(NH):
                j = h % 2
                lk = op("sync", "dma_start", kh[j][:],
                        kT_hbm[h * 128:(h + 1) * 128, :],
                        waits=[("qs0", C["qs0"]), ("qs1", C["qs1"]),
                               ("pe", kh_free[j]),
                               ("dve", entry_dve if h < 2 else 0),
                               ("pe", phase_a_pe if h < 2 else 0),
                               ("act", phase_a_act if h < 2 else 0)],
                        inc=f"kq{j}")
                lq = op("sync", "dma_start", qh[j][:],
                        qT_hbm[h * 128:(h + 1) * 128, :],
                        waits=[("pe", qh_free[j])], inc=f"kq{j}")
                SAB = [PSA, PSB]
                vv = sF[:].rearrange("p (tt c) -> p tt c", c=T)
                for tqc in range(2):
                    exp_na = [0] * 16

                    def emit_ctx(tk, h=h, tqc=tqc, exp_na=None):
                        jj = tk % 2
                        w = [("act", exp_na[tk])]
                        if tk == 0:
                            w += [cx_free[0], cx_free[1], ("dve", v_done)]
                        for half in range(2):
                            gp = op("tensor", "matmul", PS[4 + half][:],
                                    vv[:, tk, h * 128:(h + 1) * 128],
                                    pt[jj][:, half * 512:(half + 1) * 512],
                                    start=(tk == 0), stop=(tk == 15),
                                    skip_group_check=True,
                                    waits=w if half == 0 else (),
                                    inc="pe" if half == 1 else None)
                        return gp

                    for tk in range(16):
                        jj = tk % 2
                        w = [(f"kq{j}", lq), s_bank[jj]]
                        if h == 0 and tqc == 0 and tk < 2:
                            w += [bank_free_a[0], bank_free_a[1],
                                  bank_free[2], bank_free[3]]
                        for half in range(2):
                            gp = op("tensor", "matmul",
                                    SAB[jj][:, half * 512:(half + 1) * 512],
                                    kh[j][:, tk * 128:(tk + 1) * 128],
                                    qh[j][:, tqc * 1024 + half * 512:
                                           tqc * 1024 + (half + 1) * 512],
                                    start=True, stop=True, skip_group_check=True,
                                    waits=w if half == 0 else (),
                                    inc="pe" if half == 1 else None)
                        if tk >= 1:
                            ctx_gp = emit_ctx(tk - 1, exp_na=exp_na)
                            pt_free[(tk - 1) % 2] = (pt_free[(tk - 1) % 2][0],
                                                     ctx_gp)
                        ew = [("pe", gp), ("pool", pt_free[jj][0]),
                              ("pe", pt_free[jj][1])]
                        na = op("scalar", "activation", pt[jj][:], SAB[jj][:],
                                AF.Exp, scale=cSms[:], waits=ew, inc="act")
                        exp_na[tk] = na
                        s_bank[jj] = ("act", na)
                        if tk == 0:
                            np_ = op("gpsimd", "tensor_copy", sumacc[:], pt[jj][:],
                                     waits=[("act", na), ("pe", sumacc_free)],
                                     inc="pool")
                        else:
                            np_ = op("gpsimd", "tensor_add", sumacc[:], sumacc[:],
                                     pt[jj][:], waits=[("act", na)], inc="pool")
                        pt_free[jj] = (np_, pt_free[jj][1])
                    ctx_gp = emit_ctx(15, exp_na=exp_na)
                    pt_free[1] = (pt_free[1][0], ctx_gp)
                    ctx_pe = C["pe"]
                    sums_pool = C["pool"]
                    for half in range(2):
                        w = [("pool", sums_pool), p6_free]
                        npe = op("tensor", "matmul", P6[0:1, :], sOnesPF[:],
                                 sumacc[:, half * 512:(half + 1) * 512],
                                 start=True, stop=True, skip_group_check=True,
                                 waits=w, inc="pe")
                        sumacc_free = npe
                        n = op("vector", "reciprocal", rrow[:], P6[0:1, :],
                               waits=[("pe", npe)], inc="dve")
                        npe = op("tensor", "matmul", P6[:], sOnesRF[:], rrow[:],
                                 start=True, stop=True, skip_group_check=True,
                                 waits=[("dve", n)], inc="pe")
                        n = op("vector", "tensor_copy", rbcsb[:], P6[:],
                               waits=[("pe", npe)], inc="dve")
                        p6_free = ("dve", n)
                        ysl = sY[:, h * T + tqc * 1024 + half * 512:
                                 h * T + tqc * 1024 + (half + 1) * 512]
                        n = op("vector", "tensor_tensor", ysl, PS[4 + half][:],
                               rbcsb[:], ALU.mult, waits=[("pe", ctx_pe)],
                               inc="dve")
                        cx_free[half] = ("dve", n)
                        cx_free[half] = ("dve", n)
                kh_free[j] = C["pe"]
                qh_free[j] = C["pe"]
            b_end_dve = C["dve"]
            bank_free[4] = cx_free[0]
            bank_free[5] = cx_free[1]
            bank_free[6] = p6_free
            if DEBUG == "actx":
                for ft in range(KT):
                    op("gpsimd", "dma_start", outT[ft * 128:(ft + 1) * 128, :],
                       sY[:, ft * T:(ft + 1) * T], waits=[("dve", b_end_dve)],
                       inc="dg")
                return ("dve", C["dve"])

            bank_free_c = [s_bank[0], s_bank[1]]
            for of in range(OFT):
                j = of % 2
                w_ready = op("gpsimd", "dma_start", sWgu[j][:, 0:2048],
                             woutt[of], waits=[("pe", wg_free[j])], inc=f"wg{j}")
                for tc in range(NTC):
                    jj = (of * NTC + tc) % 2
                    w = [bank_free_c[jj], ("dve", b_end_dve)]
                    if tc == 0:
                        w.append((f"wg{j}", w_ready))
                    for kt in range(KT):
                        gp = op("tensor", "matmul", PS[jj][:],
                                sWgu[j][:, kt * 128:(kt + 1) * 128],
                                sY[:, kt * T + tc * 512: kt * T + tc * 512 + 512],
                                start=(kt == 0), stop=(kt == KT - 1),
                                skip_group_check=True, waits=w if kt == 0 else (),
                                inc="pe" if kt == KT - 1 else None)
                    fsl = sF[:, of * T + tc * 512: of * T + tc * 512 + 512]
                    na = op("scalar", "activation", fsl, PS[jj][:], AF.Identity,
                            bias=sOutb[:, of:of + 1],
                            waits=[("pe", gp), ("dve", b_end_dve)], inc="act")
                    bank_free_c[jj] = ("act", na)
                wg_free[j] = C["pe"]
            bank_free[0] = bank_free_c[0]
            bank_free[1] = bank_free_c[1]
            bank_free[2] = s_bank[1]
            bank_free[3] = s_bank[1]
            return ("act", C["act"])

        def dump(buf, wait):
            for ft in range(KT):
                op("gpsimd", "dma_start", outT[ft * 128:(ft + 1) * 128, :],
                   buf[:, ft * T:(ft + 1) * T], waits=[wait], inc="dg")

        y1 = boundary(xT, None, None, True, None, first=True)
        if DEBUG == "y1":
            dump(sY, ("dve", y1))
            q["sync"].append(("wait_ge", (sems["dg"], C["dg"]), {}, (), (None, None)))
        elif DEBUG in ("f1", "cbc0", "rtr"):
            f1 = ffn_stage(y1)
            if DEBUG == "f1":
                dump(sF, f1)
            q["sync"].append(("wait_ge", (sems["dg"], C["dg"]), {}, (), (None, None)))
        else:
            f1 = ffn_stage(y1)
        if DEBUG in ("y1", "f1", "cbc0", "rtr"):
            pass
        elif stages == 1:
            boundary(xT, outT, 0.5, False, f1)
        else:
            y2 = boundary(xT, x_hbm, 0.5, True, f1)
            f2 = attn_stage(y2)
            if DEBUG in ("aqk", "av", "actx"):
                q["sync"].append(("wait_ge", (sems["dg"], C["dg"]), {}, (),
                                  (None, None)))
            elif stages == 2:
                boundary(x_hbm, outT, 1.0, False, f2)
            else:
                y3 = boundary(x_hbm, x_hbm, 1.0, True, f2)
                f3 = ffn_stage(y3)
                boundary(x_hbm, outT, 0.5, False, f3)
        for sname in ("ds", "dg", "xs0", "xs1", "qs0", "qs1", "kq0", "kq1",
                      "cr", "ct", "wg0", "wg1", "wd0", "wd1"):
            if C[sname]:
                q["sync"].append(("wait_ge", (sems[sname], C[sname]), {}, (),
                                  (None, None)))

        def make_body(items):
            def body(e):
                for method, args, kw, waits, (inc, n) in items:
                    for sname, v in waits:
                        e.wait_ge(sems[sname], v)
                    if method == "wait_ge":
                        e.wait_ge(*args)
                        continue
                    inst = getattr(e, method)(*args, **kw)
                    if inc is not None:
                        inst.then_inc(sems[inc],
                                      16 if method.startswith("dma") else 1)
            return body

        for eng in ENGINES:
            getattr(blk, eng)(make_body(q[eng]))
    return nc


def prep_shared(L0_attn_Wqkv_w, L0_attn_Wqkv_b, L0_attn_out_w, L0_attn_out_b,
                L0_ffn_W_router, L0_ffn_We_gate, L0_ffn_We_up, L0_ffn_We_down,
                L0_ffn_Ws_gate, L0_ffn_Ws_up, L0_ffn_Ws_down, **_):
    f32 = lambda a: np.asarray(a, dtype=np.float32)

    def tile_in_out(w, nm, nk):
        return np.ascontiguousarray(
            w.reshape(nk, 128, nm, 128).transpose(2, 1, 0, 3).reshape(
                nm, 128, nk * 128))

    weg = f32(L0_ffn_We_gate); weu = f32(L0_ffn_We_up)
    wegu = np.empty((NE, MT, 128, 4096), np.float32)
    for e in range(NE):
        wegu[e, :, :, 0:2048] = tile_in_out(weg[e], MT, KT)
        wegu[e, :, :, 2048:4096] = tile_in_out(weu[e], MT, KT)
    wed = f32(L0_ffn_We_down)
    wedt = np.stack([tile_in_out(wed[e], OFT, MT) for e in range(NE)])
    wsg = f32(L0_ffn_Ws_gate); wsu = f32(L0_ffn_Ws_up)
    g_t = tile_in_out(wsg, 2 * MT, KT); u_t = tile_in_out(wsu, 2 * MT, KT)
    wsgu = np.empty((2, MT, 128, 4096), np.float32)
    for hh in range(2):
        wsgu[hh, :, :, 0:2048] = g_t[hh * MT:(hh + 1) * MT]
        wsgu[hh, :, :, 2048:4096] = u_t[hh * MT:(hh + 1) * MT]
    wsd = f32(L0_ffn_Ws_down)
    wsdt = np.stack([tile_in_out(wsd[hh * DE:(hh + 1) * DE], OFT, MT)
                     for hh in range(2)])
    wqkvt = tile_in_out(np.ascontiguousarray(f32(L0_attn_Wqkv_w).T), 48, KT)
    woutt = tile_in_out(np.ascontiguousarray(f32(L0_attn_out_w).T), OFT, KT)
    qkvb = np.ascontiguousarray(f32(L0_attn_Wqkv_b).reshape(48, 128).T)
    outb = np.ascontiguousarray(f32(L0_attn_out_b).reshape(16, 128).T)
    identB = np.eye(128, dtype=ml_dtypes.bfloat16)

    return dict(wegu=wegu, wedt=wedt, wsgu=wsgu, wsdt=wsdt, wqkvt=wqkvt,
                woutt=woutt, qkvb=qkvb, outb=outb,
                wr=np.ascontiguousarray(f32(L0_ffn_W_router)),
                identB=identB)


def kernel(L0_attn_hidden_states, **kw):
    x = np.asarray(L0_attn_hidden_states, dtype=np.float32)
    shared = prep_shared(**kw)
    nc = build_program(STAGES)
    in_maps = [dict(xT=np.ascontiguousarray(x[b].T), **shared) for b in range(8)]
    res = run_bass_kernel_spmd(nc, in_maps, core_ids=list(range(8)))
    out = np.stack([np.asarray(r["outT"], dtype=np.float32).T
                    for r in res.results], axis=0)
    return out
